# version 25
# baseline (speedup 1.0000x reference)
import numpy as np
import ml_dtypes
import concourse.bass as bass
import concourse.mybir as mybir
from concourse.bass_utils import run_bass_kernel_spmd
from contextlib import ExitStack

F32 = mybir.dt.float32
BF16 = mybir.dt.bfloat16
AF = mybir.ActivationFunctionType
ALU = mybir.AluOpType
AX = mybir.AxisListType

D = 2048
S = 8192
NCORE = 8
ROWS = S // NCORE
HALO = 512
XR = ROWS + HALO
NH, NKV, HD, SUBD = 16, 4, 128, 64
NQB = ROWS // 128
NKC = D // 128
NRB = XR // 128
NEG = -1.0e30
EXP_SHIFT = -4.0
NIT = NH * NQB


class Stream:

    def __init__(self, stk, nc, name):
        self.h = stk.enter_context(nc.semaphore(name))
        self.n = 0
        self.ops = []

    def bump(self, k=1):
        self.n += k
        return self.n

    def emit(self, fn):
        self.ops.append(fn)

    def take(self):
        ops, self.ops = self.ops, []
        return ops


def play(block_method, *streams):
    for st in streams:
        ops = st.take()
        if ops:
            block_method(lambda e, ops=ops: [f(e) for f in ops])


def build_program(taps=()):
    nc = bass.Bass("TRN2", target_bir_lowering=False, debug=False)

    xh = nc.dram_tensor("xh", [XR, D], F32, kind="ExternalInput").ap()
    wq = nc.dram_tensor("wq", [D, D], F32, kind="ExternalInput").ap()
    wk = nc.dram_tensor("wk", [D, NKV * HD], F32, kind="ExternalInput").ap()
    wv = nc.dram_tensor("wv", [D, NKV * HD], F32, kind="ExternalInput").ap()
    wo = nc.dram_tensor("wo", [D, D], F32, kind="ExternalInput").ap()
    wf = nc.dram_tensor("wf", [D, D], F32, kind="ExternalInput").ap()
    cosq = nc.dram_tensor("cosq", [128, ROWS], F32, kind="ExternalInput").ap()
    sinq = nc.dram_tensor("sinq", [128, ROWS], F32, kind="ExternalInput").ap()
    cosk = nc.dram_tensor("cosk", [128, XR], F32, kind="ExternalInput").ap()
    sink = nc.dram_tensor("sink", [128, XR], F32, kind="ExternalInput").ap()
    maskd = nc.dram_tensor("maskd", [128, NQB * 640], BF16, kind="ExternalInput").ap()
    ptm = nc.dram_tensor("ptm", [128, 128], BF16, kind="ExternalInput").ap()
    i128 = nc.dram_tensor("i128", [128, 128], BF16, kind="ExternalInput").ap()
    lamd = nc.dram_tensor("lamd", [128, NH], F32, kind="ExternalInput").ap()
    cstd = nc.dram_tensor("cstd", [128, 4], F32, kind="ExternalInput").ap()
    scd = nc.dram_tensor("scd", [128, D], F32, kind="ExternalInput").ap()

    y = nc.dram_tensor("y", [ROWS, D], F32, kind="ExternalOutput").ap()
    xb_dram = nc.dram_tensor("xb_dram", [XR, D], BF16, kind="Internal").ap()
    y1_dram = nc.dram_tensor("y1_dram", [ROWS, D], BF16, kind="Internal").ap()

    tap_out = {}
    tap_specs = {
        "qt": ([128, NH * ROWS], BF16), "kt": ([128, NKV * XR], BF16),
        "vn": ([128, NRB * 512], BF16), "cat": ([128, NH * ROWS], BF16),
        "y1": ([128, NQB * D], F32),
    }
    for t in taps:
        shp, dt = tap_specs[t]
        tap_out[t] = nc.dram_tensor("tap_" + t, shp, dt, kind="ExternalOutput").ap()

    stk = ExitStack()
    with stk:
        sq = Stream(stk, nc, "sq")
        pe = Stream(stk, nc, "pe")
        act = Stream(stk, nc, "act")
        dve = Stream(stk, nc, "dve")
        gx = Stream(stk, nc, "gx")
        gwkv = Stream(stk, nc, "gwkv")
        gwq = [Stream(stk, nc, "gwq0"), Stream(stk, nc, "gwq1")]
        gwos = [Stream(stk, nc, "gwos0"), Stream(stk, nc, "gwos1")]
        gwfb = Stream(stk, nc, "gwfb")
        xr = [Stream(stk, nc, "xr0"), Stream(stk, nc, "xr1")]
        xg = [Stream(stk, nc, f"xg{i}") for i in range(4)]
        bts = Stream(stk, nc, "bts")
        yst = [Stream(stk, nc, "yst0"), Stream(stk, nc, "yst1")]
        dtv = [Stream(stk, nc, "dtv0"), Stream(stk, nc, "dtv1")]
        cav = [Stream(stk, nc, "cav0"), Stream(stk, nc, "cav1")]

        ptb = stk.enter_context(nc.sbuf_tensor("ptb", [128, 128], BF16))
        idb = stk.enter_context(nc.sbuf_tensor("idb", [128, 128], BF16))
        lamb = stk.enter_context(nc.sbuf_tensor("lamb", [128, NH], F32))
        cstb = stk.enter_context(nc.sbuf_tensor("cstb", [128, 4], F32))
        lamn = stk.enter_context(nc.sbuf_tensor("lamn", [128, NH], F32))

        with nc.Block() as block:
            st_mks = ExitStack()
            mks = st_mks.enter_context(nc.sbuf_tensor("mks", [128, NQB * 640], BF16))
            st_qkv = ExitStack()
            qt = st_qkv.enter_context(nc.sbuf_tensor("qt", [128, NH * ROWS], BF16))
            ktz1 = st_qkv.enter_context(nc.sbuf_tensor("ktz1", [128, NKV * XR], BF16))
            ktz2 = st_qkv.enter_context(nc.sbuf_tensor("ktz2", [128, NKV * XR], BF16))
            vn = st_qkv.enter_context(nc.sbuf_tensor("vn", [128, NRB * 512], BF16))

            st1 = ExitStack()
            xbt = st1.enter_context(nc.sbuf_tensor("xbt", [128, NKC * XR], BF16))
            wkb = st1.enter_context(nc.sbuf_tensor("wkb", [128, NKC * 512], BF16))
            wvb = st1.enter_context(nc.sbuf_tensor("wvb", [128, NKC * 512], BF16))
            wqs = [st1.enter_context(nc.sbuf_tensor(f"wqs{i}", [128, NKC * 128], BF16)) for i in range(2)]
            cqb = st1.enter_context(nc.sbuf_tensor("cqb", [128, ROWS], F32))
            sqb = st1.enter_context(nc.sbuf_tensor("sqb", [128, ROWS], F32))
            ckb = st1.enter_context(nc.sbuf_tensor("ckb", [128, XR], F32))
            skb = st1.enter_context(nc.sbuf_tensor("skb", [128, XR], F32))
            pbf = [st1.enter_context(nc.sbuf_tensor(f"pbf{i}", [128, 512], BF16)) for i in range(2)]
            rpa = [st1.enter_context(nc.sbuf_tensor(f"rpa{i}", [128, 512], F32)) for i in range(2)]
            rpb = [st1.enter_context(nc.sbuf_tensor(f"rpb{i}", [128, 512], F32)) for i in range(2)]
            ps_pj = [st1.enter_context(nc.psum_tensor(f"ps_pj{i}", [128, 512], F32)) for i in range(3)]
            ps_rt = [st1.enter_context(nc.psum_tensor(f"ps_rt{i}", [128, 512], F32)) for i in range(2)]

            def boot(e):
                for dst, src in [(ptb, ptm), (idb, i128), (lamb, lamd), (cstb, cstd), (cqb, cosq),
                                 (sqb, sinq), (ckb, cosk), (skb, sink), (mks, maskd)]:
                    e.dma_start(out=dst[:], in_=src[:, :]).then_inc(bts.h, 16)
            bts.bump(16 * 9)
            sq_bootv = bts.n
            act.emit(boot)

            def g_loads(e):
                for g in range(4):
                    e.dma_start(out=xb_dram[:, g * 512:(g + 1) * 512],
                                in_=xh[:, g * 512:(g + 1) * 512]).then_inc(gx.h, 16)
                for wdst, wsrc in [(wkb, wk), (wvb, wv)]:
                    for kc in range(NKC):
                        e.dma_start(out=wdst[:, kc * 512:(kc + 1) * 512],
                                    in_=wsrc[kc * 128:(kc + 1) * 128, :]).then_inc(gwkv.h, 16)
            gx.bump(16 * 4)
            gwkv.bump(16 * 2 * NKC)
            gwkv_total = gwkv.n
            gx.emit(g_loads)

            wq_done = [0] * NH

            def emit_load_wq(h, wait_pe):
                sem = gwq[h % 2]
                wq_done[h] = sem.bump(16 * 4)

                def f(e, h=h, wait_pe=wait_pe, sem=sem):
                    if wait_pe:
                        e.wait_ge(pe.h, wait_pe)
                    dst = wqs[h % 2]
                    for g in range(4):
                        ksl = slice(g * 512, (g + 1) * 512)
                        src = wq[ksl, h * 128:(h + 1) * 128].rearrange("(kc p) o -> p kc o", p=128)
                        dsl = dst[:, g * 512:(g + 1) * 512].rearrange("p (kc o) -> p kc o", o=128)
                        e.dma_start(out=dsl, in_=src).then_inc(sem.h, 16)
                gx.emit(f)

            emit_load_wq(0, 0)
            emit_load_wq(1, 0)

            def xbt_loads_q(e):
                for g in range(4):
                    e.wait_ge(gx.h, 16 * (g + 1))
                    for kc in range(4 * g, 4 * g + 4):
                        e.dma_start_transpose(xbt[:, kc * XR:(kc + 1) * XR],
                                              xb_dram[:, kc * 128:(kc + 1) * 128]).then_inc(xg[g].h, 16)
            for g in range(4):
                xg[g].bump(16 * 4)
            sq.emit(xbt_loads_q)

            dve.bump()

            def dve_boot(e):
                e.wait_ge(bts.h, sq_bootv)
                ins = e.tensor_scalar(out=lamn[:], in0=lamb[:], scalar1=-1.0, scalar2=None, op0=ALU.mult)
                ins.then_inc(dve.h, 1)
                e.drain()
            dve.emit(dve_boot)

            ops1 = [("K", o, ch) for o in range(NKV) for ch in range(3)]
            ops1 += [("V", rb, 0) for rb in range(NRB)]
            ops1 += [("Q", h, ch) for h in range(NH) for ch in range(2)]
            NOP1 = len(ops1)
            evac_done = [0] * NOP1
            rot_done = [0] * NOP1
            rope_done = [0] * NOP1
            proj_done = [0] * NOP1

            def rope_src_done(j):
                return evac_done[j] if ops1[j][0] == "V" else rope_done[j]

            def emit_proj(i):
                kind, a, ch = ops1[i]
                par = i % 3
                proj_done[i] = pe.bump()

                def pe_proj(e, i=i, kind=kind, a=a, ch=ch, par=par):
                    if i >= 3:
                        e.wait_ge(dve.h, rope_src_done(i - 3))
                    if kind == "K":
                        e.wait_ge(gwkv.h, gwkv_total)
                        for kc in range(NKC):
                            if kc % 4 == 0 and i < 4:
                                e.wait_ge(xg[kc // 4].h, 64)
                            last = e.matmul(ps_pj[par][:, :],
                                            wkb[:, kc * 512 + a * 128: kc * 512 + (a + 1) * 128],
                                            xbt[:, kc * XR + ch * 512: kc * XR + (ch + 1) * 512],
                                            start=(kc == 0), stop=(kc == NKC - 1))
                    elif kind == "V":
                        e.wait_ge(gwkv.h, gwkv_total)
                        for kc in range(NKC):
                            if kc % 4 == 0 and i < 4:
                                e.wait_ge(xg[kc // 4].h, 64)
                            last = e.matmul(ps_pj[par][:, :],
                                            xbt[:, kc * XR + a * 128: kc * XR + (a + 1) * 128],
                                            wvb[:, kc * 512:(kc + 1) * 512],
                                            start=(kc == 0), stop=(kc == NKC - 1))
                    else:
                        e.wait_ge(gwq[a % 2].h, wq_done[a])
                        for kc in range(NKC):
                            if kc % 4 == 0 and i < 4:
                                e.wait_ge(xg[kc // 4].h, 64)
                            last = e.matmul(ps_pj[par][:, :],
                                            wqs[a % 2][:, kc * 128:(kc + 1) * 128],
                                            xbt[:, kc * XR + HALO + ch * 512: kc * XR + HALO + (ch + 1) * 512],
                                            start=(kc == 0), stop=(kc == NKC - 1))
                    last.then_inc(pe.h, 1)
                pe.emit(pe_proj)

            def emit_rot(i):
                par3 = i % 3
                rot_done[i] = pe.bump()

                def pe_rot(e, i=i):
                    e.wait_ge(dve.h, evac_done[i])
                    if i >= 2 and ops1[i - 2][0] != "V":
                        e.wait_ge(dve.h, rope_done[i - 2])
                    ins = e.matmul(ps_rt[i % 2][:, :], ptb[:, :], pbf[i % 2][:, :], start=True, stop=True)
                    ins.then_inc(pe.h, 1)
                pe.emit(pe_rot)

            def emit_evac(i):
                kind, a, ch = ops1[i]
                par = i % 3
                evac_done[i] = dve.bump()

                def dve_evac(e, i=i, kind=kind, a=a, par=par):
                    e.wait_ge(pe.h, proj_done[i])
                    if kind == "V":
                        ins = e.tensor_copy(vn[:, a * 512:(a + 1) * 512], ps_pj[par][:, :])
                    else:
                        if i >= 2 and ops1[i - 2][0] != "V":
                            e.wait_ge(pe.h, rot_done[i - 2])
                        ins = e.tensor_copy(pbf[i % 2][:], ps_pj[par][:, :])
                    ins.then_inc(dve.h, 1)
                dve.emit(dve_evac)

            def emit_rope(i):
                kind, a, ch = ops1[i]
                par = i % 3
                if kind == "K":
                    dve.bump()
                rope_done[i] = dve.bump()

                def dve_rope(e, i=i, kind=kind, a=a, ch=ch, par=par):
                    e.wait_ge(pe.h, rot_done[i])
                    cols = slice(ch * 512, (ch + 1) * 512)
                    if kind == "K":
                        cos_, sin_ = ckb, skb
                    else:
                        cos_, sin_ = cqb, sqb
                    e.tensor_tensor(out=rpa[i % 2][:], in0=ps_pj[par][:, :], in1=cos_[:, cols], op=ALU.mult)
                    e.tensor_tensor(out=rpb[i % 2][:], in0=ps_rt[i % 2][:, :], in1=sin_[:, cols], op=ALU.mult)
                    e.drain()
                    if kind == "K":
                        kcols = slice(a * XR + ch * 512, a * XR + (ch + 1) * 512)
                        ins = e.tensor_tensor(out=ktz1[0:64, kcols], in0=rpa[i % 2][0:64, :],
                                              in1=rpb[i % 2][0:64, :], op=ALU.add)
                        ins.then_inc(dve.h, 1)
                        ins = e.tensor_tensor(out=ktz2[64:128, kcols], in0=rpa[i % 2][64:128, :],
                                              in1=rpb[i % 2][64:128, :], op=ALU.add)
                        ins.then_inc(dve.h, 1)
                    else:
                        dst = qt[:, a * ROWS + ch * 512: a * ROWS + (ch + 1) * 512]
                        ins = e.tensor_tensor(out=dst, in0=rpa[i % 2][:], in1=rpb[i % 2][:], op=ALU.add)
                        ins.then_inc(dve.h, 1)
                dve.emit(dve_rope)

            def emit_ktz_zero():
                dve.bump()
                dve.bump()

                def dve_kz(e):
                    ins = e.tensor_scalar(out=ktz1[64:128, :], in0=ktz2[64:128, :],
                                          scalar1=0.0, scalar2=None, op0=ALU.mult)
                    ins.then_inc(dve.h, 1)
                    ins = e.tensor_scalar(out=ktz2[0:64, :], in0=ktz1[0:64, :],
                                          scalar1=0.0, scalar2=None, op0=ALU.mult)
                    ins.then_inc(dve.h, 1)
                dve.emit(dve_kz)

            for i in range(NOP1 + 1):
                if i == 13:
                    emit_ktz_zero()
                if i < NOP1:
                    emit_proj(i)
                    emit_evac(i)
                if i >= 1 and ops1[i - 1][0] != "V":
                    emit_rot(i - 1)
                    emit_rope(i - 1)
                if i < NOP1 and ops1[i][0] == "Q" and ops1[i][2] == 1 and ops1[i][1] + 2 < NH:
                    emit_load_wq(ops1[i][1] + 2, proj_done[i])

            ph1_pe, ph1_dve = pe.n, dve.n

            if any(t in tap_out for t in ("qt", "kt", "vn")):
                sq.bump(16 * sum(1 for t in ("qt", "kt", "vn") if t in tap_out))

                def ph1_taps(e):
                    e.wait_ge(dve.h, ph1_dve)
                    for t, buf in (("qt", qt), ("kt", ktz1), ("vn", vn)):
                        if t in tap_out:
                            e.dma_start(out=tap_out[t][:, :], in_=buf[:]).then_inc(sq.h, 16)
                sq.emit(ph1_taps)

            st1.close()

            st_cat = ExitStack()
            cat = st_cat.enter_context(nc.sbuf_tensor("cat", [128, NH * ROWS], BF16, side="right"))
            st_ow = ExitStack()
            wos = [st_ow.enter_context(nc.sbuf_tensor(f"wos{i}", [128, NKC * 512], BF16, side="right")) for i in range(2)]
            st2 = ExitStack()
            e1b = st2.enter_context(nc.sbuf_tensor("e1b", [128, 3 * 640], F32))
            e2b = st2.enter_context(nc.sbuf_tensor("e2b", [128, 3 * 640], F32))
            g0b = st2.enter_context(nc.sbuf_tensor("g0b", [128, 2 * 640], F32))
            db = st2.enter_context(nc.sbuf_tensor("db", [128, 2 * 640], BF16))
            dnb = st2.enter_context(nc.sbuf_tensor("dnb", [128, 2 * 640], BF16))
            dts = st2.enter_context(nc.sbuf_tensor("dts", [128, 2 * 640], BF16))
            sm1 = st2.enter_context(nc.sbuf_tensor("sm1", [128, 2], F32))
            sm2 = st2.enter_context(nc.sbuf_tensor("sm2", [128, 2], F32))
            tb = st2.enter_context(nc.sbuf_tensor("tb", [128, 48], F32))
            ps_s1 = st2.enter_context(nc.psum_tensor("ps_s1", [128, 640], F32))
            ps_s2 = st2.enter_context(nc.psum_tensor("ps_s2", [128, 640], F32))
            ps_dt = st2.enter_context(nc.psum_tensor("ps_dt", [128, 640], F32))
            ps_av = st2.enter_context(nc.psum_tensor("ps_av", [128, 128], F32))

            play(block.sync, sq)
            play(block.gpsimd, gx)
            play(block.tensor, pe)
            play(block.vector, dve)

            wos_done = [0] * 4

            def emit_load_wos(dchunk, wait_pe, first=False):
                sem = gwos[dchunk % 2]
                wos_done[dchunk] = sem.bump(16 * NKC)

                def f(e, dchunk=dchunk, wait_pe=wait_pe, first=first, sem=sem):
                    if first:
                        e.wait_ge(dve.h, ph1_dve)
                    e.wait_ge(pe.h, wait_pe)
                    dst = wos[dchunk % 2]
                    for kc in range(NKC):
                        e.dma_start(out=dst[:, kc * 512:(kc + 1) * 512],
                                    in_=wo[kc * 128:(kc + 1) * 128, dchunk * 512:(dchunk + 1) * 512]).then_inc(sem.h, 16)
                gx.emit(f)

            emit_load_wos(0, ph1_pe, first=True)
            emit_load_wos(1, ph1_pe)

            scores_done = [0] * NIT
            exp1_done = [0] * NIT
            exps_done = [0] * NIT
            tp_done = [0] * NIT
            dtevac_done = [0] * NIT
            av_done = [0] * NIT
            avevac_done = [0] * NIT
            NB = NIT + 5
            dn_ev = [0] * NB
            front_ev = [0] * NB

            def hq(it):
                return it // NQB, it % NQB

            for it in range(NIT + 8):
                if it < NIT:
                    scores_done[it] = pe.bump()
                if 6 <= it < NIT + 6:
                    tp_done[it - 6] = pe.bump()
                if 7 <= it < NIT + 7:
                    av_done[it - 7] = pe.bump()

                def pe2(e, it=it):
                    if it < NIT:
                        h, qb = hq(it)
                        kv = h // 4
                        if it == 0:
                            e.wait_ge(dve.h, ph1_dve)
                            e.wait_ge(bts.h, sq_bootv)
                        mc = qb * 640
                        kbase = kv * XR + qb * 128
                        for si, (ps, ktz) in enumerate(((ps_s1, ktz1), (ps_s2, ktz2))):
                            if it > 0:
                                e.wait_ge(act.h, exp1_done[it - 1] if si == 0 else exps_done[it - 1])
                            qsl = qt[0:128, h * ROWS + qb * 128: h * ROWS + (qb + 1) * 128]
                            e.matmul(ps[:, 0:128], idb[:, :], mks[:, mc:mc + 128], start=True, stop=False)
                            if qb <= 2:
                                e.matmul(ps[:, 128:512], idb[:, :], mks[:, mc + 128:mc + 512], start=False, stop=False)
                            e.matmul(ps[:, 512:640], idb[:, :], mks[:, mc + 512:mc + 640], start=True, stop=False)
                            e.matmul(ps[:, 0:512], qsl, ktz[0:128, kbase:kbase + 512], start=False, stop=True)
                            last = e.matmul(ps[:, 512:640], qsl, ktz[0:128, kbase + 512:kbase + 640], start=False, stop=True)
                        last.then_inc(pe.h, 1)
                    if 6 <= it < NIT + 6:
                        j = it - 6
                        e.wait_ge(dve.h, dn_ev[j + 5])
                        if j >= 1:
                            e.wait_ge(act.h, dtevac_done[j - 1])
                        pj = j % 2
                        for k in range(5):
                            last = e.matmul(ps_dt[:, k * 128:(k + 1) * 128],
                                            dnb[:, pj * 640 + k * 128: pj * 640 + (k + 1) * 128],
                                            idb[:, :], start=True, stop=True)
                        last.then_inc(pe.h, 1)
                    if 7 <= it < NIT + 7:
                        j = it - 7
                        h, qb = hq(j)
                        kv = h // 4
                        e.wait_ge(act.h, dtevac_done[j])
                        if j >= 1:
                            e.wait_ge(act.h, avevac_done[j - 1])
                        pj = j % 2
                        for k in range(5):
                            last = e.matmul(ps_av[:, :],
                                            vn[:, (qb + k) * 512 + kv * 128: (qb + k) * 512 + (kv + 1) * 128],
                                            dts[:, pj * 640 + k * 128: pj * 640 + (k + 1) * 128],
                                            start=(k == 0), stop=(k == 4))
                        last.then_inc(pe.h, 1)
                pe.emit(pe2)

                if it < NIT:
                    exp1_done[it] = act.bump()
                    exps_done[it] = act.bump()
                if 6 <= it < NIT + 6:
                    dtevac_done[it - 6] = act.bump()
                if 7 <= it < NIT + 7:
                    avevac_done[it - 7] = act.bump()

                def act2(e, it=it):
                    if it < NIT:
                        par = it % 2
                        p3 = it % 3
                        e.wait_ge(pe.h, scores_done[it])
                        if it >= 1:
                            e.wait_ge(dve.h, front_ev[it - 1])
                        ins = e.activation(out=e1b[:, p3 * 640:(p3 + 1) * 640], in_=ps_s1[:, :], func=AF.Exp,
                                     bias=cstb[:, 0:1], accum_out=sm1[:, par:par + 1])
                        ins.then_inc(act.h, 1)
                        ins = e.activation(out=e2b[:, p3 * 640:(p3 + 1) * 640], in_=ps_s2[:, :], func=AF.Exp,
                                           bias=cstb[:, 0:1], accum_out=sm2[:, par:par + 1])
                        ins.then_inc(act.h, 1)
                act.emit(act2)

                def act2b(e, it=it):
                    if 6 <= it < NIT + 6:
                        j = it - 6
                        pj = j % 2
                        e.wait_ge(pe.h, tp_done[j])
                        if j >= 2:
                            e.wait_ge(pe.h, av_done[j - 2])
                        ins = e.copy(out=dts[:, pj * 640:(pj + 1) * 640], in_=ps_dt[:, :])
                        ins.then_inc(act.h, 1)
                    if 7 <= it < NIT + 7:
                        j = it - 7
                        hj, qbj = hq(j)
                        e.wait_ge(pe.h, av_done[j])
                        ins = e.copy(out=cat[:, hj * ROWS + qbj * 128: hj * ROWS + (qbj + 1) * 128], in_=ps_av[:, :])
                        ins.then_inc(act.h, 1)
                act.emit(act2b)

                if it < NB:
                    dn_ev[it] = dve.bump()
                    front_ev[it] = dve.bump()

                def dve2(e, j=it):
                    if j >= NB:
                        return
                    if j >= 8:
                        e.wait_ge(pe.h, tp_done[j - 7])
                    fr = []
                    if 0 <= j - 5 < NIT:
                        i6 = j - 5
                        b = (i6 % 6) * 8
                        p2 = i6 % 2
                        fr.append(lambda e, i6=i6, b=b, p2=p2: e.tensor_scalar(
                            out=dnb[:, p2 * 640:(p2 + 1) * 640], in0=db[:, p2 * 640:(p2 + 1) * 640],
                            scalar1=tb[:, b + 5:b + 6], scalar2=None, op0=ALU.mult))
                    if 0 <= j - 4 < NIT:
                        i5 = j - 4
                        b = (i5 % 6) * 8
                        fr.append(lambda e, b=b: e.reciprocal(out=tb[:, b + 5:b + 6], in_=tb[:, b + 3:b + 4]))
                    if 0 <= j - 3 < NIT:
                        i4 = j - 3
                        b = (i4 % 6) * 8
                        p2 = i4 % 2
                        fr.append(lambda e, b=b, p2=p2: e.tensor_scalar(
                            out=db[:, p2 * 640:(p2 + 1) * 640], in0=g0b[:, p2 * 640:(p2 + 1) * 640],
                            scalar1=0.0, scalar2=tb[:, b + 6:b + 7], op0=ALU.max,
                            op1=ALU.add, accum_out=tb[:, b + 3:b + 4]))
                    if 0 <= j - 2 < NIT:
                        i3 = j - 2
                        b = (i3 % 6) * 8
                        p2 = i3 % 2
                        p3 = i3 % 3
                        fr.append(lambda e, b=b, p2=p2, p3=p3: e.scalar_tensor_tensor(
                            out=g0b[:, p2 * 640:(p2 + 1) * 640], in0=e2b[:, p3 * 640:(p3 + 1) * 640],
                            scalar=tb[:, b + 2:b + 3], in1=e1b[:, p3 * 640:(p3 + 1) * 640],
                            op0=ALU.mult, op1=ALU.add))
                    if 0 <= j - 1 < NIT:
                        i2 = j - 1
                        b = (i2 % 6) * 8
                        par = i2 % 2
                        h2 = hq(i2)[0]
                        fr.append(lambda e, b=b, par=par, h2=h2: e.scalar_tensor_tensor(
                            out=tb[:, b + 2:b + 3], in0=tb[:, b + 0:b + 1], scalar=lamn[:, h2:h2 + 1],
                            in1=sm1[:, par:par + 1], op0=ALU.mult, op1=ALU.mult))
                    last = None
                    for i, f in enumerate(fr):
                        ins = f(e)
                        if len(fr) >= 2:
                            if i == 0:
                                ins.then_inc(dve.h, 1)
                            if i == len(fr) - 1:
                                ins.then_inc(dve.h, 1)
                        elif len(fr) == 1:
                            ins.then_inc(dve.h, 2)
                        last = ins
                    if j < NIT:
                        par = j % 2
                        b = (j % 6) * 8
                        e.wait_ge(act.h, exps_done[j])
                        ins = e.reciprocal(out=tb[:, b + 0:b + 1], in_=sm2[:, par:par + 1])
                        if not fr:
                            ins.then_inc(dve.h, 2)
                        e.tensor_scalar(out=tb[:, b + 6:b + 7], in0=sm1[:, par:par + 1],
                                        scalar1=1e-6, scalar2=None, op0=ALU.mult)
                    e.drain()
                dve.emit(dve2)

            ph2_pe, ph2_act, ph2_dve = pe.n, act.n, dve.n

            if "cat" in tap_out:
                sq.bump(16)

                def ph2_taps(e):
                    e.wait_ge(dve.h, ph2_dve)
                    e.dma_start(out=tap_out["cat"][:, :], in_=cat[:]).then_inc(sq.h, 16)
                sq.emit(ph2_taps)

            play(block.sync, sq)
            play(block.gpsimd, gx)
            play(block.tensor, pe)
            play(block.scalar, act)
            play(block.vector, dve)

            st2.close()
            st_qkv.close()
            st_mks.close()

            stOF = ExitStack()
            y1 = stOF.enter_context(nc.sbuf_tensor("y1", [128, NQB * D], F32))
            sqscr = stOF.enter_context(nc.sbuf_tensor("sqscr", [128, D], BF16))
            tyO = stOF.enter_context(nc.sbuf_tensor("tyO", [128, 48], F32))
            wfb = stOF.enter_context(nc.sbuf_tensor("wfb", [128, NKC * D], BF16))
            stO = ExitStack()
            xres = [stO.enter_context(nc.sbuf_tensor(f"xres{i}", [128, 512], F32)) for i in range(2)]
            y1bf = [stO.enter_context(nc.sbuf_tensor("y1bf0", [128, D], BF16))] * 2
            ps_o = [stO.enter_context(nc.psum_tensor(f"ps_o{i}", [128, 512], F32)) for i in range(2)]

            gwfb.bump(16 * NKC)
            gwfb_total = gwfb.n

            def load_wfb(e):
                e.wait_ge(dve.h, ph2_dve)
                e.wait_ge(act.h, ph2_act)
                for kc in range(NKC):
                    e.dma_start(out=wfb[:, kc * D:(kc + 1) * D],
                                in_=wf[kc * 128:(kc + 1) * 128, :]).then_inc(gwfb.h, 16)
            gx.emit(load_wfb)

            o_done = [[0] * NQB for _ in range(4)]
            h1_done = [[0] * NQB for _ in range(4)]
            xres_done = [[0] * NQB for _ in range(4)]

            for dchunk in range(4):
                if dchunk >= 2:
                    emit_load_wos(dchunk, o_done[dchunk - 2][NQB - 1])
                for qb in range(NQB):
                    idx = dchunk * NQB + qb
                    xres_done[dchunk][qb] = xr[qb % 2].bump(16)

                    def sq_x(e, d=dchunk, qb=qb, idx=idx):
                        if idx <= 1:
                            e.wait_ge(act.h, ph2_act)
                            e.wait_ge(dve.h, ph2_dve)
                        if idx >= 2:
                            pd, pq = divmod(idx - 2, NQB)
                            e.wait_ge(dve.h, h1_done[pd][pq])
                        e.dma_start(out=xres[qb % 2][:],
                                    in_=xh[HALO + qb * 128: HALO + (qb + 1) * 128,
                                           d * 512:(d + 1) * 512]).then_inc(xr[qb % 2].h, 16)
                    sq.emit(sq_x)

                    o_done[dchunk][qb] = pe.bump()

                    def peO(e, d=dchunk, qb=qb, idx=idx):
                        if idx == 0:
                            e.wait_ge(act.h, avevac_done[NIT - 1])
                        e.wait_ge(gwos[d % 2].h, wos_done[d])
                        if idx >= 2:
                            pd, pq = divmod(idx - 2, NQB)
                            e.wait_ge(dve.h, h1_done[pd][pq])
                        for h in range(NH):
                            last = e.matmul(ps_o[qb % 2][:, :],
                                            cat[:, h * ROWS + qb * 128: h * ROWS + (qb + 1) * 128],
                                            wos[d % 2][:, h * 512:(h + 1) * 512],
                                            start=(h == 0), stop=(h == NH - 1))
                        last.then_inc(pe.h, 1)
                    pe.emit(peO)

                    h1_done[dchunk][qb] = dve.bump()

                    def dveO(e, d=dchunk, qb=qb):
                        e.wait_ge(pe.h, o_done[d][qb])
                        e.wait_ge(xr[qb % 2].h, xres_done[d][qb])
                        ins = e.tensor_tensor(out=y1[:, qb * D + d * 512: qb * D + (d + 1) * 512],
                                              in0=ps_o[qb % 2][:, :], in1=xres[qb % 2][:], op=ALU.add)
                        ins.then_inc(dve.h, 1)
                    dve.emit(dveO)

            ln1_norm_done = [0] * NQB
            y1bf_done = [0] * NQB
            y1dram_done = [0] * NQB
            sqacc_done = [0] * NQB
            var_done = [0] * NQB
            rstd_done = [0] * NQB

            def em_actL(qb):
                sqacc_done[qb] = act.bump()

                def f(e, qb=qb):
                    e.wait_ge(dve.h, h1_done[3][qb])
                    ins = e.activation(out=sqscr[:], in_=y1[:, qb * D:(qb + 1) * D], func=AF.Square,
                                       accum_out=tyO[:, qb:qb + 1])
                    ins.then_inc(act.h, 1)
                act.emit(f)

            def em_dveL(qb):
                var_done[qb] = dve.bump()

                def f(e, qb=qb):
                    e.wait_ge(act.h, sqacc_done[qb])
                    e.drain()
                    e.reduce_sum(out=tyO[:, 40:41], in_=y1[:, qb * D:(qb + 1) * D], axis=AX.X)
                    e.drain()
                    e.tensor_scalar(out=tyO[:, 8 + qb:9 + qb], in0=tyO[:, 40:41],
                                    scalar1=1.0 / D, scalar2=None, op0=ALU.mult)
                    e.tensor_scalar(out=tyO[:, 41:42], in0=tyO[:, qb:qb + 1], scalar1=1.0 / D, scalar2=None, op0=ALU.mult)
                    e.drain()
                    e.tensor_tensor(out=tyO[:, 42:43], in0=tyO[:, 8 + qb:9 + qb],
                                    in1=tyO[:, 8 + qb:9 + qb], op=ALU.mult)
                    e.drain()
                    ins = e.tensor_tensor(out=tyO[:, 24 + qb:25 + qb], in0=tyO[:, 41:42], in1=tyO[:, 42:43], op=ALU.subtract)
                    ins.then_inc(dve.h, 1)
                dve.emit(f)

            def em_actL2(qb):
                rstd_done[qb] = act.bump()

                def f(e, qb=qb):
                    e.wait_ge(dve.h, var_done[qb])
                    e.activation(out=tyO[:, 32 + qb:33 + qb], in_=tyO[:, 24 + qb:25 + qb], func=AF.Ln, bias=cstb[:, 1:2])
                    e.drain()
                    ins = e.activation(out=tyO[:, 16 + qb:17 + qb], in_=tyO[:, 32 + qb:33 + qb],
                                       func=AF.Exp, scale=-0.5)
                    ins.then_inc(act.h, 1)
                act.emit(f)

            def em_dveL2(qb):
                ln1_norm_done[qb] = dve.bump()

                def f(e, qb=qb):
                    e.wait_ge(act.h, rstd_done[qb])
                    ins = e.tensor_scalar(out=y1[:, qb * D:(qb + 1) * D], in0=y1[:, qb * D:(qb + 1) * D],
                                          scalar1=tyO[:, 8 + qb:9 + qb],
                                          scalar2=tyO[:, 16 + qb:17 + qb],
                                          op0=ALU.subtract, op1=ALU.mult)
                    ins.then_inc(dve.h, 1)
                dve.emit(f)

            def em_actL3(qb):
                y1bf_done[qb] = act.bump()

                def f(e, qb=qb):
                    e.wait_ge(dve.h, ln1_norm_done[qb])
                    if qb >= 1:
                        e.wait_ge(sq.h, y1dram_done[qb - 1])
                    ins = e.copy(out=y1bf[qb % 2][:], in_=y1[:, qb * D:(qb + 1) * D])
                    ins.then_inc(act.h, 1)
                act.emit(f)

            def em_sqL(qb):
                y1dram_done[qb] = sq.bump(16)

                def f(e, qb=qb):
                    e.wait_ge(act.h, y1bf_done[qb])
                    e.dma_start(out=y1_dram[qb * 128:(qb + 1) * 128, :], in_=y1bf[qb % 2][:]).then_inc(sq.h, 16)
                sq.emit(f)

            for qb in range(NQB):
                em_actL(qb)
            for qb in range(NQB):
                em_dveL(qb)
                em_actL2(qb)
            for qb in range(NQB):
                em_dveL2(qb)
                em_actL3(qb)
                em_sqL(qb)

            phO_pe = pe.n

            play(block.sync, sq)
            play(block.gpsimd, gx)
            play(block.tensor, pe)
            play(block.scalar, act)
            play(block.vector, dve)

            st_ow.close()
            st_cat.close()
            stO.close()

            stF = ExitStack()
            y1t = stF.enter_context(nc.sbuf_tensor("y1t", [128, NKC * ROWS], BF16))
            scb = stF.enter_context(nc.sbuf_tensor("scb", [128, D], F32))
            ffs = [stF.enter_context(nc.sbuf_tensor(f"ffs{i}", [128, D], F32)) for i in range(2)]
            y2b = [stF.enter_context(nc.sbuf_tensor(f"y2b{i}", [128, D], F32)) for i in range(2)]
            tyF = stF.enter_context(nc.sbuf_tensor("tyF", [128, 48], F32))
            ps_f = [stF.enter_context(nc.psum_tensor(f"ps_f{i}", [128, 512], F32)) for i in range(4)]

            sq.bump(16 * 1)
            y1t_grp = [0] * 4
            for g in range(4):
                y1t_grp[g] = xg[g].bump(16 * 4)

            def sqF0(e):
                e.wait_ge(sq.h, y1dram_done[NQB - 1])
                e.wait_ge(pe.h, phO_pe)
                e.dma_start(out=scb[:], in_=scd[:, :]).then_inc(sq.h, 16)
                for kc in range(NKC):
                    e.dma_start_transpose(y1t[:, kc * ROWS:(kc + 1) * ROWS],
                                          y1_dram[:, kc * 128:(kc + 1) * 128]).then_inc(xg[kc // 4].h, 16)
            sq.emit(sqF0)

            psf_done = [[0] * 4 for _ in range(NQB)]
            ffs_done = [[0] * 4 for _ in range(NQB)]
            store_done = [0] * NQB
            y2_done = [0] * NQB
            stats_done = [0] * NQB
            v0_done = [0] * NQB
            vn_done = [0] * NQB
            t_done = [0] * NQB
            sq2_done = [0] * NQB
            varF_done = [0] * NQB
            rstdF_done = [0] * NQB

            def em_MM(qb):
                for kchunk in range(4):
                    idx = qb * 4 + kchunk
                    psf_done[qb][kchunk] = pe.bump()

                    def peF(e, qb=qb, kchunk=kchunk, idx=idx):
                        if idx == 0:
                            e.wait_ge(gwfb.h, gwfb_total)
                        if idx >= 4:
                            pq, pk = divmod(idx - 4, 4)
                            e.wait_ge(dve.h, ffs_done[pq][pk])
                        for kc in range(NKC):
                            if kc % 4 == 0 and idx < 4:
                                e.wait_ge(xg[kc // 4].h, y1t_grp[kc // 4])
                            last = e.matmul(ps_f[kchunk % 4][:, :],
                                            y1t[:, kc * ROWS + qb * 128: kc * ROWS + (qb + 1) * 128],
                                            wfb[:, kc * D + kchunk * 512: kc * D + (kchunk + 1) * 512],
                                            start=(kc == 0), stop=(kc == NKC - 1))
                        last.then_inc(pe.h, 1)
                    pe.emit(peF)

                    ffs_done[qb][kchunk] = dve.bump()

                    def dveF(e, qb=qb, kchunk=kchunk):
                        e.wait_ge(pe.h, psf_done[qb][kchunk])
                        if qb >= 2 and kchunk == 0:
                            e.wait_ge(yst[qb % 2].h, store_done[qb - 2])
                        ins = e.tensor_tensor(out=ffs[qb % 2][:, kchunk * 512:(kchunk + 1) * 512],
                                              in0=ps_f[kchunk % 4][:, :],
                                              in1=scb[:, kchunk * 512:(kchunk + 1) * 512], op=ALU.mult)
                        ins.then_inc(dve.h, 1)
                    dve.emit(dveF)

            def em_CH(qb):
                P0 = 24 * (qb % 2)

                def col(i):
                    return tyF[:, P0 + i:P0 + i + 1]

                v0_done[qb] = act.bump()

                def actF(e, qb=qb):
                    e.wait_ge(dve.h, ffs_done[qb][3])
                    e.activation(out=sqscr[:], in_=ffs[qb % 2][:], func=AF.Square, accum_out=col(0))
                    e.drain()
                    e.activation(out=col(1), in_=col(0), func=AF.Ln, bias=cstb[:, 2:3])
                    e.drain()
                    ins = e.activation(out=col(2), in_=col(1), func=AF.Exp, scale=0.5)
                    ins.then_inc(act.h, 1)
                act.emit(actF)

                vn_done[qb] = dve.bump()

                def dveFt(e, qb=qb):
                    e.wait_ge(act.h, v0_done[qb])
                    ins = e.tensor_scalar(out=col(3), in0=col(2), scalar1=1e-7, scalar2=None, op0=ALU.max)
                    ins.then_inc(dve.h, 1)
                dve.emit(dveFt)

                t_done[qb] = act.bump()

                def actF2(e, qb=qb):
                    e.wait_ge(dve.h, vn_done[qb])
                    ins = e.activation(out=col(4), in_=col(3), func=AF.Exp, scale=-2.0)
                    ins.then_inc(act.h, 1)
                act.emit(actF2)

                stats_done[qb] = dve.bump()

                def dveF2(e, qb=qb):
                    e.wait_ge(act.h, t_done[qb])
                    e.drain()
                    e.tensor_scalar(out=col(5), in0=col(4), scalar1=-1.0, scalar2=1.0, op0=ALU.mult, op1=ALU.add)
                    e.tensor_scalar(out=col(6), in0=col(4), scalar1=1.0, scalar2=None, op0=ALU.add)
                    e.tensor_scalar(out=col(7), in0=col(3), scalar1=1e-7, scalar2=None, op0=ALU.add)
                    e.reciprocal(out=col(8), in_=col(2))
                    e.drain()
                    e.tensor_tensor(out=col(9), in0=col(6), in1=col(7), op=ALU.mult)
                    e.tensor_scalar(out=col(10), in0=col(8), scalar1=0.99, scalar2=None, op0=ALU.mult)
                    e.drain()
                    e.reciprocal(out=col(11), in_=col(9))
                    e.drain()
                    e.tensor_tensor(out=col(12), in0=col(5), in1=col(11), op=ALU.mult)
                    e.drain()
                    e.tensor_tensor(out=col(13), in0=col(10), in1=col(12), op=ALU.min)
                    e.drain()
                    e.scalar_tensor_tensor(out=y2b[qb % 2][:], in0=ffs[qb % 2][:], scalar=col(13),
                                           in1=y1[:, qb * D:(qb + 1) * D], op0=ALU.mult, op1=ALU.add)
                    e.drain()
                    e.reduce_sum(out=col(14), in_=y2b[qb % 2][:], axis=AX.X)
                    e.drain()
                    e.tensor_scalar(out=col(15), in0=col(14), scalar1=1.0 / D, scalar2=None, op0=ALU.mult)
                    e.drain()
                    ins = e.tensor_tensor(out=col(16), in0=col(15), in1=col(15), op=ALU.mult)
                    ins.then_inc(dve.h, 1)
                dve.emit(dveF2)

                sq2_done[qb] = act.bump()

                def actF3(e, qb=qb):
                    e.wait_ge(dve.h, stats_done[qb])
                    ins = e.activation(out=sqscr[:], in_=y2b[qb % 2][:], func=AF.Square, accum_out=col(17))
                    ins.then_inc(act.h, 1)
                act.emit(actF3)

                varF_done[qb] = dve.bump()

                def dveF3(e, qb=qb):
                    e.wait_ge(act.h, sq2_done[qb])
                    e.drain()
                    e.tensor_scalar(out=col(18), in0=col(17), scalar1=1.0 / D, scalar2=None, op0=ALU.mult)
                    e.drain()
                    ins = e.tensor_tensor(out=col(19), in0=col(18), in1=col(16), op=ALU.subtract)
                    ins.then_inc(dve.h, 1)
                dve.emit(dveF3)

                rstdF_done[qb] = act.bump()

                def actF4(e, qb=qb):
                    e.wait_ge(dve.h, varF_done[qb])
                    e.activation(out=col(20), in_=col(19), func=AF.Ln, bias=cstb[:, 1:2])
                    e.drain()
                    ins = e.activation(out=col(21), in_=col(20), func=AF.Exp, scale=-0.5)
                    ins.then_inc(act.h, 1)
                act.emit(actF4)

                y2_done[qb] = dve.bump()

                def dveF4(e, qb=qb):
                    e.wait_ge(act.h, rstdF_done[qb])
                    ins = e.tensor_scalar(out=y2b[qb % 2][:], in0=y2b[qb % 2][:],
                                          scalar1=col(15), scalar2=col(21),
                                          op0=ALU.subtract, op1=ALU.mult)
                    ins.then_inc(dve.h, 1)
                dve.emit(dveF4)

                store_done[qb] = yst[qb % 2].bump(16)

                def sqF(e, qb=qb):
                    e.wait_ge(dve.h, y2_done[qb])
                    e.dma_start(out=y[qb * 128:(qb + 1) * 128, :], in_=y2b[qb % 2][:]).then_inc(yst[qb % 2].h, 16)
                sq.emit(sqF)

            for qb in range(NQB + 1):
                if qb >= 1:
                    em_CH(qb - 1)
                if qb < NQB:
                    em_MM(qb)

            if "y1" in tap_out:
                sq.bump(16)

            def sq_end(e):
                if "y1" in tap_out:
                    e.wait_ge(dve.h, ln1_norm_done[NQB - 1])
                    e.dma_start(out=tap_out["y1"][:, :], in_=y1[:]).then_inc(sq.h, 16)
                e.wait_ge(sq.h, sq.n)
                e.wait_ge(yst[0].h, yst[0].n)
                e.wait_ge(yst[1].h, yst[1].n)
            sq.emit(sq_end)

            play(block.sync, sq)
            play(block.gpsimd, gx)
            play(block.tensor, pe)
            play(block.scalar, act)
            play(block.vector, dve)

            stF.close()
            stOF.close()

    return nc, tap_out



def make_core_inputs(x_full, Wq, Wk, Wv, Wo, Wf, lam, scale):
    theta = 10000.0
    inv = 1.0 / theta ** (np.arange(0, HD, 2, dtype=np.float64) / HD)
    invd = inv[np.arange(128) % 64]

    PT = np.zeros((128, 128), np.float32)
    for m in range(64):
        PT[m + 64, m] = -1.0
    for m in range(64, 128):
        PT[m - 64, m] = 1.0
    PTb = PT.astype(ml_dtypes.bfloat16)
    I = np.eye(128, dtype=ml_dtypes.bfloat16)

    in_maps = []
    for c in range(NCORE):
        own0 = c * ROWS
        xcore = np.zeros((XR, D), np.float32)
        lo = own0 - HALO
        src_lo = max(lo, 0)
        xcore[src_lo - lo:, :] = x_full[src_lo: own0 + ROWS, :]

        pq = np.arange(own0, own0 + ROWS, dtype=np.float64)
        cq = (np.cos(invd[:, None] * pq[None, :]) * 0.125).astype(np.float32)
        sq_ = (np.sin(invd[:, None] * pq[None, :]) * 0.125).astype(np.float32)
        pk = np.arange(lo, lo + XR, dtype=np.float64)
        ck = np.cos(invd[:, None] * pk[None, :]).astype(np.float32)
        sk = np.sin(invd[:, None] * pk[None, :]).astype(np.float32)

        m = np.zeros((128, NQB * 640), np.float32)
        for qb in range(NQB):
            qg = own0 + qb * 128 + np.arange(128)[:, None]
            kg = own0 + qb * 128 - 512 + np.arange(640)[None, :]
            allowed = (kg <= qg) & (kg >= qg - 511) & (kg >= 0)
            m[:, qb * 640:(qb + 1) * 640] = np.where(allowed, 0.0, NEG)
        mb = m.astype(ml_dtypes.bfloat16)

        in_maps.append({
            "xh": xcore, "wq": Wq, "wk": Wk, "wv": Wv, "wo": Wo, "wf": Wf,
            "cosq": cq, "sinq": sq_, "cosk": ck, "sink": sk,
            "maskd": mb, "ptm": PTb, "i128": I,
            "lamd": np.ascontiguousarray(np.broadcast_to(lam.astype(np.float32), (128, NH))),
            "cstd": np.ascontiguousarray(np.broadcast_to(
                np.array([EXP_SHIFT, 1e-5, 1e-30, 0.0], np.float32), (128, 4))),
            "scd": np.ascontiguousarray(np.broadcast_to(scale.astype(np.float32), (128, D))),
        })
    return in_maps


def kernel(L0_attn_x, L0_attn_Wq, L0_attn_Wk, L0_attn_Wv, L0_attn_Wo, L0_attn_lam,
           L0_ffn_weight, L0_ffn_scale, taps=()):
    x = np.asarray(L0_attn_x, np.float32).reshape(S, D)
    in_maps = make_core_inputs(x, np.asarray(L0_attn_Wq, np.float32), np.asarray(L0_attn_Wk, np.float32),
                               np.asarray(L0_attn_Wv, np.float32), np.asarray(L0_attn_Wo, np.float32),
                               np.asarray(L0_ffn_weight, np.float32), np.asarray(L0_attn_lam, np.float32),
                               np.asarray(L0_ffn_scale, np.float32))
    nc, tap_out = build_program(taps=taps)
    res = run_bass_kernel_spmd(nc, in_maps, core_ids=list(range(NCORE)))
    out = np.concatenate([res.results[c]["y"] for c in range(NCORE)], axis=0).reshape(1, S, D)
    if taps:
        return out, [{k: res.results[c]["tap_" + k] for k in taps} for c in range(NCORE)]
    return out
